# version 05 (final)
# baseline (speedup 1.0000x reference)
import numpy as np
import ml_dtypes
import concourse.bass as bass
import concourse.mybir as mybir
from concourse.bass_utils import run_bass_kernel_spmd

BFNP = ml_dtypes.bfloat16
F32 = mybir.dt.float32
BF16 = mybir.dt.bfloat16
FP16 = mybir.dt.float16
Alu = mybir.AluOpType
Act = mybir.ActivationFunctionType
AXX = mybir.AxisListType.X

N = 8192
DM = 2048
DF = 8192
H = 16
HD = 128
CH = 128
T = 2048
NCH = 16
NF = 16
NO = 64
MAGIC = 12582912.0
CONSTS = [1e-5, 1e-6, -0.5, 1.0 / 2048.0, 1.0 / 128.0, 1.0 / (2048.0 * 8192.0)]

DEBUG = []
PHASES = 99.0
ENGINES = ("sync", "vector", "scalar", "tensor", "gpsimd")

_TAB = None


def _tables():
    global _TAB
    if _TAB is not None:
        return _TAB
    import jax
    with jax.default_device(jax.devices("cpu")[0]):
        import jax.numpy as jnp
        half = HD // 2
        theta = 10000.0 ** (-2.0 * jnp.arange(half, dtype=jnp.float32) / HD)
        ang = jnp.arange(N, dtype=jnp.float32)[:, None] * theta[None, :]
        cos = np.asarray(jnp.cos(ang))
        sin = np.asarray(jnp.sin(ang))
        slopes = 2.0 ** (-8.0 * (jnp.arange(H, dtype=jnp.float32) + 1.0) / H)
        log_lam = np.asarray(-slopes)
        i = jnp.arange(CH, dtype=jnp.float32)
        diff = i[:, None] - i[None, :]
        ll = jnp.asarray(log_lam)
        D = np.asarray(jnp.where(diff[None] >= 0, jnp.exp(ll[:, None, None] * diff[None]), 0.0))
        dq = np.asarray(jnp.exp(ll[None, :] * (i[:, None] + 1.0)))
        dk = np.asarray(jnp.exp(ll[None, :] * (CH - 1.0 - i[:, None])))
        lam_c = np.asarray(jnp.exp(ll * float(CH)))
    crot = np.empty((128, N), np.float32); srot = np.empty((128, N), np.float32)
    crot[:64] = cos.T; crot[64:] = cos.T
    srot[:64] = -sin.T; srot[64:] = sin.T
    dt_mask = np.empty((128, H * CH), np.float32)
    dqt = np.empty((128, H * CH), np.float32)
    for h in range(H):
        dt_mask[:, h * CH:(h + 1) * CH] = D[h].T
        dqt[:, h * CH:(h + 1) * CH] = dq[:, h][None, :]
    permT = np.zeros((128, 128), np.float32)
    for m in range(128):
        permT[(m + 64) % 128, m] = 1.0
    stwb = np.zeros((4, 4, 128, H * CH), np.float32)
    for p in range(4):
        for j in range(p):
            for h in range(H):
                w = np.float32(np.float32(lam_c[h]) ** (16 * (p - 1 - j)))
                stwb[p, j, :, h * CH:(h + 1) * CH] = w
    _TAB = dict(crot=crot, srot=srot, dt_mask=dt_mask, dqt=dqt,
                dkt=np.ascontiguousarray(dk.astype(np.float32)),
                lam_c=lam_c.astype(np.float32), permT=permT.astype(BFNP), stwb=stwb,
                ident=np.eye(128, dtype=np.float32),
                identb=np.eye(128, dtype=np.float32).astype(BFNP))
    return _TAB


class Prog:
    def __init__(self):
        self.ops = {e: [] for e in ENGINES}
        self.cnt = {}
        self.sems = {}
        self.wmark = {}
        self.rguard = {}

    def inc(self, sem, by=1):
        self.cnt[sem] = self.cnt.get(sem, 0) + by
        return self.cnt[sem]

    def now(self, sem):
        return self.cnt.get(sem, 0)

    def emit(self, eng, fn):
        self.ops[eng].append(fn)

    def wait(self, eng, sem, val):
        if not val:
            return
        assert val <= self.now(sem), f"wait {sem}>={val} built={self.now(sem)}"
        self.emit(eng, lambda E, s=sem, v=val: E.wait_ge(self.sems[s], v))

    def dma(self, eng, out, in_, sem, slow=False):
        v = self.inc(sem, 16)
        self.emit(eng, lambda E, o=out, i=in_, s=sem, sl=slow:
                  E.dma_start(out=o, in_=i, allow_slow_non_contiguous=sl)
                  .then_inc(self.sems[s], 16))
        return v

    def op(self, eng, fn, sem=None):
        if sem is None:
            self.emit(eng, fn)
            return None
        v = self.inc(sem, 1)
        self.emit(eng, lambda E, f=fn, s=sem: f(E).then_inc(self.sems[s], 1))
        return v

    def wr(self, eng, key):
        for (s, v) in self.rguard.get(key, []):
            self.wait(eng, s, v)
        self.rguard[key] = []

    def wrote(self, key, sem, val):
        self.wmark[key] = (sem, val)

    def rd(self, eng, key):
        m = self.wmark.get(key)
        if m:
            self.wait(eng, m[0], m[1])

    def red(self, key, sem, val):
        self.rguard.setdefault(key, []).append((sem, val))


class Arena:
    def __init__(self, nc, name, base, size):
        self.nc, self.name, self.base, self.size = nc, name, base, size
        self.off = 0
        self.n = 0

    def reset(self, to=0):
        self.off = to

    def take(self, shape, dtype, name=None):
        per = int(np.prod(shape[1:])) * mybir.dt.size(dtype)
        per = (per + 31) // 32 * 32
        assert self.off + per <= self.size, f"arena {self.name}: {self.off}+{per}>{self.size}"
        self.n += 1
        t = self.nc.alloc_sbuf_tensor_at(name or f"{self.name}_t{self.n}",
                                         list(shape), dtype, offset=self.base + self.off)
        self.off += per
        return t


def build_nc():
    tab = _tables()
    lam_c = [float(x) for x in tab["lam_c"]]
    nc = bass.Bass("TRN2", target_bir_lowering=False, detect_race_conditions=False)

    def din(name, shape, dt=F32):
        return nc.dram_tensor(name, list(shape), dt, kind="ExternalInput").ap()

    xT = din("xT", [DM, T])
    wq = din("wq", [DM, DM]); wk = din("wk", [DM, DM]); wv = din("wv", [DM, DM])
    wo_d = din("wo", [DM, DM])
    wg1 = din("wg1", [DM, HD]); wg2 = din("wg2", [HD, DM])
    nwt = din("nwt", [128, NF])
    wgT = din("wgT", [DM, DF]); wuT = din("wuT", [DM, DF]); wdT = din("wdT", [DF, DM])
    cvec = din("cvec", [128, NF])
    wmod_sl = din("wmod_sl", [DM, 3072])
    crot_t = din("crot_t", [128, T]); srot_t = din("srot_t", [128, T])
    dt_mask_t = din("dt_mask_t", [128, H * CH]); dqt_t = din("dqt_t", [128, H * CH])
    dkt_t = din("dkt_t", [128, H])
    permT_t = din("permT_t", [128, 128], BF16)
    idf_t = din("idf_t", [128, 128], F32)
    idb_t = din("idb_t", [128, 128], BF16)
    stwb_t = din("stwb_t", [4, 128, H * CH])
    cstt = din("cstt", [128, 8])

    yT = nc.dram_tensor("yT", [DM, T], F32, kind="ExternalOutput").ap()

    def dint(name, shape, dt, out=False):
        return nc.dram_tensor(name, list(shape), dt,
                              kind="ExternalOutput" if out else "Internal").ap()

    dbg = lambda tag: tag in DEBUG
    q_sp = dint("q_sp", [4, NF, 128, 512], BF16, out=dbg("qkv"))
    k_sp = dint("k_sp", [4, NF, 128, 512], BF16, out=dbg("qkv"))
    vf_sp = dint("vf_sp", [4, NF, 128, 512], BF16, out=dbg("qkv"))
    g_sp = dint("g_sp", [4, NF, 128, 512], BF16, out=dbg("qkv"))
    kd_sp = dint("kd_sp", [NCH, 128, DM], BF16, out=dbg("acc"))
    vt_sp = dint("vt_sp", [NCH, 128, DM], BF16, out=dbg("acc"))
    attn_sp = dint("attn_sp", [NF, 128, T], F32, out=dbg("attn"))
    ccm_in = dint("ccm_in", [1, 3072], F32)
    ccm_out = dint("ccm_out", [4, 3072], F32)
    ccs_in = dint("ccs_in", [128, H * CH], F32)
    acc_dbg = dint("acc_dbg", [128, H * CH], F32, out=True) if dbg("acc") else None
    ccs_out = dint("ccs_out", [4, 128, H * CH], F32)
    h_dbg = dint("h_dbg", [128, NF * T], BF16, out=True) if dbg("h") else None
    f_dbg1 = dint("f_dbg1", [128, NF * 512], F32, out=True) if dbg("ffn") else None
    f_dbg2 = dint("f_dbg2", [128, NF * 512], BF16, out=True) if dbg("ffn") else None
    f_dbg3 = dint("f_dbg3", [128, NO * 512], FP16, out=True) if dbg("ffn") else None
    f_dbg4 = dint("f_dbg4", [128, 4 * 512 + 16], F32, out=True) if dbg("ffn") else None
    si_dbg = dint("si_dbg", [128, H * CH], F32, out=True) if dbg("acc") else None

    BASE = 0x5000
    AR1 = Arena(nc, "AR1", BASE + 0x00000, 0x10000)
    AR2 = Arena(nc, "AR2", BASE + 0x10000, 0x04000)
    AR3 = Arena(nc, "AR3", BASE + 0x14000, 0x08000)
    AR4 = Arena(nc, "AR4", BASE + 0x1C000, 0x08000)
    AR5 = Arena(nc, "AR5", BASE + 0x24000, 0x04000)
    PERS = Arena(nc, "PERS", BASE + 0x28000, 0xAFE0)
    assert BASE + 0x28000 + 0xAFE0 <= 229344

    ca1 = PERS.take([128, NF], F32, "ca1"); sa_v = PERS.take([128, NF], F32, "sa_v")
    ga_v = PERS.take([128, NF], F32, "ga_v")
    cb1 = PERS.take([128, NF], F32, "cb1"); sb_v = PERS.take([128, NF], F32, "sb_v")
    gb_v = PERS.take([128, NF], F32, "gb_v")
    nw_s = PERS.take([128, NF], F32, "nw_s")
    cvec_s = PERS.take([128, NF], F32, "cvec_s")
    cvec_b = PERS.take([128, NF], F32, "cvec_b")
    gbswd = PERS.take([128, NF], F32, "gbswd")
    dkt_s = PERS.take([128, H], F32, "dkt_s")
    ones_b = PERS.take([128, 1], BF16, "ones_b")
    ones_f = PERS.take([128, 1], F32, "ones_f")
    onesr_f = PERS.take([1, 128], F32, "onesr_f")
    permT_s = PERS.take([128, 128], BF16, "permT_s")
    idf_s = PERS.take([128, 128], F32, "idf_s")
    idb_s = PERS.take([128, 128], BF16, "idb_s")
    acc_s = PERS.take([128, H * CH], F32, "acc_s")
    sbf_s = PERS.take([128, H * CH], BF16, "sbf_s")
    row_a = PERS.take([1, 512], F32, "row_a")
    row_b = PERS.take([1, 512], F32, "row_b")
    row_c = PERS.take([1, 512], F32, "row_c")
    mb_s = PERS.take([128, 512], F32, "mb_s")
    rb_s = PERS.take([128, 512], F32, "rb_s")
    amax_s = PERS.take([128, 512], F32, "amax_s")
    sxb_s = PERS.take([128, 512], F32, "sxb_s")
    sxi_s = PERS.take([128, 512], F32, "sxi_s")
    psc0 = PERS.take([128, 512], F32, "psc0")
    psc1 = PERS.take([128, 512], F32, "psc1")
    scb0 = PERS.take([128, 512], BF16, "scb0")
    scb1 = PERS.take([128, 512], BF16, "scb1")
    colsc = PERS.take([128, 8], F32, "colsc")
    colrows = PERS.take([8, 128], F32, "colrows")
    swg_c = PERS.take([128, 1], F32, "swg_c"); swu_c = PERS.take([128, 1], F32, "swu_c")
    swd_c = PERS.take([128, 1], F32, "swd_c")
    rwg_c = PERS.take([128, 1], F32, "rwg_c"); rwu_c = PERS.take([128, 1], F32, "rwu_c")
    rwd_c = PERS.take([128, 1], F32, "rwd_c")
    macc = PERS.take([128, 4], F32, "macc")
    mred = PERS.take([128, 1], F32, "mred")
    srow = PERS.take([1, 4], F32, "srow")
    tmp_s = PERS.take([128, NF], F32, "tmp_s")
    tmp_s2 = PERS.take([128, NF], F32, "tmp_s2")
    cst_s = PERS.take([128, 8], F32, "cst_s")
    assert PERS.off <= PERS.size, PERS.off
    for j, val in enumerate(CONSTS):
        nc.const_aps.register(F32, float(val), cst_s[:, j:j + 1])

    def at4(off, shape, dt, name):
        return nc.alloc_sbuf_tensor_at(name, list(shape), dt, offset=AR4.base + off)

    def at3(off, shape, dt, name):
        return nc.alloc_sbuf_tensor_at(name, list(shape), dt, offset=AR3.base + off)

    mtile = [at4(0x6000 + i * 0x1000, [128, 1024], F32, f"mtile{i}") for i in range(2)]

    pb = [nc.alloc_psum_tensor(f"pb{i}", [128, 512], F32) for i in range(6)]
    pf = [nc.alloc_psum_tensor(f"pf{i}", [128, 1024], BF16) for i in range(2)]

    P = Prog()
    SEMS = ["qG", "eT", "eV", "eS", "eG", "cc",
            "sK", "sW0", "sW1", "sX", "sM0", "sM1", "sB0", "sB1", "sR",
            "sC0", "sC1", "sO", "sP0", "sP1", "sG0", "sG1", "sD0", "sD1",
            "aR0", "aR1", "aK", "aV", "aW0", "aW1", "aY"]

    def v3(t, a, bsz):
        return t[:].rearrange("p (a b) -> p a b", a=a)

    BK = [f"bank{i}" for i in range(6)] + ["pfA", "pfB"]

    def mean_tile_list():
        out = []
        for col, w in ((0, wgT), (1, wuT), (2, wdT)):
            rblocks = DM // 128 if col < 2 else DF // 128
            cblocks = (DF if col < 2 else DM) // 1024
            for r in range(rblocks):
                for cc2 in range(cblocks):
                    out.append((col, w, r, cc2))
        return out

    mean_items = mean_tile_list()
    mean_pos = [0]
    mean_marks = {0: None, 1: None, 2: None}

    def drip_mean(k=1):
        for _ in range(k):
            if mean_pos[0] >= len(mean_items):
                return
            idx = mean_pos[0]
            col, w, r, cc2 = mean_items[idx]
            sl = idx % 2
            P.wr("sync", f"mt{sl}")
            lv = P.dma("sync", mtile[sl][:],
                       w[r * 128:(r + 1) * 128, cc2 * 1024:(cc2 + 1) * 1024], f"sM{sl}")
            P.wrote(f"mt{sl}", f"sM{sl}", lv)
            P.rd("vector", f"mt{sl}")
            rsv = P.op("vector", lambda E, sl=sl: E.tensor_reduce(
                out=mred[:], in_=mtile[sl][:], axis=AXX, op=Alu.add,
                apply_absolute_value=True), "eV")
            P.wait("vector", "eV", rsv)
            mv = P.op("vector", lambda E, col=col: E.tensor_tensor(
                out=macc[:, col:col + 1], in0=macc[:, col:col + 1], in1=mred[:],
                op=Alu.add), "eV")
            P.red(f"mt{sl}", "eV", mv)
            mean_marks[col] = mv
            mean_pos[0] += 1

    for dst, src in ((nw_s, nwt), (cvec_s, cvec), (dkt_s, dkt_t), (permT_s, permT_t),
                     (idf_s, idf_t), (idb_s, idb_t), (cst_s, cstt)):
        P.dma("sync", dst[:], src, "sK")
    ld_consts = P.now("sK")

    def _init_v(E):
        E.memset(ones_b[:], 1.0)
        E.memset(ones_f[:], 1.0)
        E.memset(onesr_f[:], 1.0)
        E.memset(acc_s[:], 0.0)
        return E.memset(macc[:], 0.0)
    init_v = P.op("vector", _init_v, "eV")
    for i in range(6):
        P.wrote(BK[i], "eV", init_v)
    P.wrote("init", "eV", init_v)

    P.wait("scalar", "sK", ld_consts)
    ss_cv = P.op("scalar", lambda E: E.activation(out=cvec_b[:], in_=cvec_s[:], func=Act.Silu), "eS")

    wm_land = [AR3.take([128, 3072], F32, f"wm_land{i}") for i in range(2)]
    for f in range(NF):
        P.wr("sync", f"wm{f % 2}")
        lv = P.dma("sync", wm_land[f % 2][:], wmod_sl[f * 128:(f + 1) * 128, :], f"sW{f % 2}")
        P.wait("tensor", f"sW{f % 2}", lv)
        if f == 0:
            P.wait("tensor", "eS", ss_cv)
            P.wait("tensor", "eV", init_v)
        for nt in range(5):
            P.op("tensor", lambda E, f=f, nt=nt: E.matmul(
                pb[nt][0:1, 0:512], cvec_b[:, f:f + 1],
                wm_land[f % 2][:, nt * 512:(nt + 1) * 512],
                start=(f == 0), stop=(f == NF - 1)))
        mv = P.op("tensor", lambda E, f=f: E.matmul(
            pb[5][0:1, 0:512], cvec_b[:, f:f + 1],
            wm_land[f % 2][:, 5 * 512:6 * 512],
            start=(f == 0), stop=(f == NF - 1)), "eT")
        P.red(f"wm{f % 2}", "eT", mv)
    mod_mm = mv
    P.wait("scalar", "eT", mod_mm)
    for nt in range(6):
        cpv = P.op("scalar", lambda E, nt=nt: E.copy(out=row_a[0:1, 0:512],
                                                     in_=pb[nt][0:1, 0:512]), "eS")
        P.wait("gpsimd", "eS", cpv)
        dmv = P.dma("gpsimd", ccm_in[0:1, nt * 512:(nt + 1) * 512], row_a[0:1, 0:512], "qG")
        P.wait("scalar", "qG", dmv)
        P.red(BK[nt], "eS", cpv)
    P.wait("gpsimd", "qG", P.now("qG"))
    ccv = P.op("gpsimd", lambda E: E.collective_compute(
        "AllGather", Alu.bypass, replica_groups=[[0, 1, 2, 3], [4, 5, 6, 7]],
        ins=[ccm_in[:].opt()], outs=[ccm_out[:].opt()]), "cc")
    P.wait("gpsimd", "cc", ccv)
    ccm_flat = ccm_out.rearrange("g n -> (g n)")
    for tsb, j in ((sa_v, 0), (ca1, 1), (ga_v, 2), (sb_v, 3), (cb1, 4), (gb_v, 5)):
        gvv = P.dma("gpsimd", tsb[:],
                    ccm_flat[j * DM:(j + 1) * DM].rearrange("(c p) -> p c", p=128), "qG",
                    slow=True)
    P.wait("vector", "qG", gvv)
    def vstep(fn):
        v = P.op("vector", fn, "eV")
        P.wait("vector", "eV", v)
        return v
    vstep(lambda E: E.tensor_scalar_add(out=ca1[:], in0=ca1[:], scalar1=1.0))
    vstep(lambda E: E.tensor_scalar_add(out=cb1[:], in0=cb1[:], scalar1=1.0))
    for gsb in (ga_v, gb_v):
        vstep(lambda E, g=gsb: E.tensor_scalar(
            out=tmp_s[:], in0=g[:], scalar1=0.0, scalar2=2.0, op0=Alu.is_ge, op1=Alu.mult))
        vstep(lambda E: E.tensor_scalar(
            out=tmp_s[:], in0=tmp_s[:], scalar1=1.0, scalar2=None, op0=Alu.subtract))
        vstep(lambda E, g=gsb: E.tensor_scalar_mul(out=tmp_s2[:], in0=g[:], scalar1=-1.0))
        vstep(lambda E, g=gsb: E.tensor_tensor(out=tmp_s2[:], in0=g[:], in1=tmp_s2[:],
                                               op=Alu.max))
        vstep(lambda E: E.tensor_scalar(
            out=tmp_s2[:], in0=tmp_s2[:], scalar1=0.25, scalar2=None, op0=Alu.max))
        mod_ready = vstep(lambda E, g=gsb: E.tensor_mul(
            out=g[:], in0=tmp_s[:], in1=tmp_s2[:]))

    h_sb = AR1.take([128, NF * T], BF16, "h_sb")
    x0t = AR2.take([128, NF * 256], F32, "x0t")
    crot_s = at3(0x0000, [128, T], F32, "crot_s")
    srot_s = at3(0x2000, [128, T], F32, "srot_s")
    ar3_scr = 0x4000
    AR3.reset(ar3_scr)
    sqb = AR3.take([128, NF * 256], BF16, "sqb")

    P.wait("sync", "eT", mod_mm)
    P.dma("sync", crot_s[:], crot_t, "sR")
    ld_rot = P.dma("sync", srot_s[:], srot_t, "sR")

    for tt in range(8):
        P.wr("sync", "x0t")
        ldx = P.dma("sync", v3(x0t, NF, 256),
                    xT.rearrange("(f p) t -> p f t", p=128)[:, :, tt * 256:(tt + 1) * 256], "sX")
        P.wrote("x0t", "sX", ldx)
        drip_mean(2)
        P.rd("vector", "x0t")
        P.wr("vector", f"hcol{tt}")
        cast_v = P.op("vector", lambda E, tt=tt: E.tensor_copy(
            out=v3(h_sb, NF, T)[:, :, tt * 256:(tt + 1) * 256], in_=v3(x0t, NF, 256)), "eV")
        P.rd("scalar", "x0t")
        P.wr("scalar", "sqb")
        sq_v = P.op("scalar", lambda E: E.activation(out=sqb[:], in_=x0t[:], func=Act.Square), "eS")
        P.wrote("sqb", "eS", sq_v)
        P.wait("tensor", "eV", cast_v)
        P.wait("tensor", "eS", sq_v)
        P.wr("tensor", BK[0]); P.wr("tensor", BK[1])
        for f in range(NF):
            P.op("tensor", lambda E, f=f, tt=tt: E.matmul(
                pb[0][0:1, 0:256], ones_b[:], v3(h_sb, NF, T)[:, f, tt * 256:(tt + 1) * 256],
                start=(f == 0), stop=(f == NF - 1)))
        for f in range(NF - 1):
            P.op("tensor", lambda E, f=f: E.matmul(
                pb[1][0:1, 0:256], ones_b[:], v3(sqb, NF, 256)[:, f, :],
                start=(f == 0), stop=False))
        st_v = P.op("tensor", lambda E: E.matmul(
            pb[1][0:1, 0:256], ones_b[:], v3(sqb, NF, 256)[:, NF - 1, :],
            start=False, stop=True), "eT")
        P.red("sqb", "eT", st_v)
        P.red(f"hcol{tt}", "eT", st_v)
        P.wait("scalar", "eT", st_v)
        P.op("scalar", lambda E: E.copy(out=row_a[0:1, 0:256], in_=pb[0][0:1, 0:256]))
        cpr = P.op("scalar", lambda E: E.copy(out=row_b[0:1, 0:256], in_=pb[1][0:1, 0:256]), "eS")
        P.red(BK[0], "eS", cpr); P.red(BK[1], "eS", cpr)
        P.op("scalar", lambda E: E.mul(out=row_a[0:1, 0:256], in_=row_a[0:1, 0:256], mul=1.0 / DM))
        rr = P.op("scalar", lambda E: E.activation(out=row_c[0:1, 0:256], in_=row_a[0:1, 0:256],
                                                   func=Act.Square), "eS")
        P.wait("vector", "eS", rr)
        var_v = P.op("vector", lambda E: E.scalar_tensor_tensor(
            out=row_b[0:1, 0:256], in0=row_b[0:1, 0:256], scalar=1.0 / DM,
            in1=row_c[0:1, 0:256], op0=Alu.mult, op1=Alu.subtract), "eV")
        P.wait("scalar", "eV", var_v)
        P.op("scalar", lambda E: E.activation(out=row_b[0:1, 0:256], in_=row_b[0:1, 0:256],
                                              func=Act.Ln, bias=1e-5))
        rstd_v = P.op("scalar", lambda E: E.activation(out=row_b[0:1, 0:256],
                                                       in_=row_b[0:1, 0:256],
                                                       func=Act.Exp, scale=-0.5), "eS")
        P.wait("tensor", "eS", rstd_v)
        P.wr("tensor", BK[2]); P.wr("tensor", BK[3])
        P.op("tensor", lambda E: E.matmul(pb[2][:, 0:256], onesr_f[0:1, :], row_a[0:1, 0:256],
                                          start=True, stop=True))
        bc_v = P.op("tensor", lambda E: E.matmul(pb[3][:, 0:256], onesr_f[0:1, :],
                                                 row_b[0:1, 0:256], start=True, stop=True), "eT")
        P.wait("vector", "eT", bc_v)
        P.wr("vector", "mbrb")
        P.op("vector", lambda E: E.tensor_copy(out=mb_s[:, 0:256], in_=pb[2][:, 0:256]))
        cmb = P.op("vector", lambda E: E.tensor_copy(out=rb_s[:, 0:256], in_=pb[3][:, 0:256]), "eV")
        P.red(BK[2], "eV", cmb); P.red(BK[3], "eV", cmb)
        if tt == 0:
            P.wait("vector", "eV", mod_ready)
        P.wr("vector", f"hcol{tt}")
        for f in range(NF):
            xf = v3(x0t, NF, 256)[:, f, :]
            P.op("vector", lambda E, xf=xf: E.tensor_sub(out=xf, in0=xf, in1=mb_s[:, 0:256]))
            P.op("vector", lambda E, xf=xf: E.tensor_mul(out=xf, in0=xf, in1=rb_s[:, 0:256]))
            md = P.op("vector", lambda E, xf=xf, f=f, tt=tt: E.tensor_scalar(
                out=v3(h_sb, NF, T)[:, f, tt * 256:(tt + 1) * 256], in0=xf,
                scalar1=ca1[:, f:f + 1], scalar2=sa_v[:, f:f + 1], op0=Alu.mult, op1=Alu.add),
                "eV" if f == NF - 1 else None)
        P.red("x0t", "eV", md)
        P.red("mbrb", "eV", md)
        if tt == 7:
            P.wrote("h", "eV", md)
    if h_dbg is not None:
        P.rd("gpsimd", "h")
        P.dma("gpsimd", h_dbg, h_sb[:], "qG")
    if PHASES <= 1:
        return _finalize(nc, P, SEMS)

    AR3.reset(ar3_scr)
    qsil = [AR3.take([128, 512], BF16, f"qsil{i}") for i in range(2)]
    r1_s = AR3.take([128, 512], F32, "r1_s")
    r2_s = AR3.take([128, 512], F32, "r2_s")
    rout = [AR3.take([128, 512], BF16, f"rout{i}") for i in range(2)]
    land = [at4(i * 0x2000, [128, NF * 128], F32, f"land{i}") for i in range(2)]
    wbf = [at4(0x4000 + i * 0x1000, [128, NF * 128], BF16, f"wbf{i}") for i in range(2)]

    P.wait("vector", "sR", ld_rot)

    def proj_phase(wsrc, dst_sp, rot, silu):
        it = 0
        for g in range(NF):
            P.wr("sync", f"land{g % 2}")
            ldb = P.dma("sync", v3(land[g % 2], NF, 128),
                        wsrc[:, g * 128:(g + 1) * 128].rearrange("(c p) g2 -> p c g2", p=128),
                        f"sB{g % 2}")
            P.wrote(f"land{g % 2}", f"sB{g % 2}", ldb)
            drip_mean(2)
            P.rd("vector", f"land{g % 2}")
            P.wr("vector", f"wbf{g % 2}")
            cs = P.op("vector", lambda E, g=g: E.tensor_copy(
                out=wbf[g % 2][:], in_=land[g % 2][:]), "eV")
            P.red(f"land{g % 2}", "eV", cs)
            P.wrote(f"wbf{g % 2}", "eV", cs)
            for tt in range(4):
                bk = tt % 2
                P.rd("tensor", f"wbf{g % 2}")
                if g == 0 and tt == 0:
                    P.rd("tensor", "h")
                P.wr("tensor", BK[bk])
                mmv = None
                for f in range(NF):
                    mmv = P.op("tensor", lambda E, g=g, tt=tt, f=f, bk=bk: E.matmul(
                        pb[bk][:, 0:512], wbf[g % 2][:, f * 128:(f + 1) * 128],
                        v3(h_sb, NF, T)[:, f, tt * 512:(tt + 1) * 512],
                        start=(f == 0), stop=(f == NF - 1)),
                        "eT" if f == NF - 1 else None)
                if tt == 3:
                    P.red(f"wbf{g % 2}", "eT", mmv)
                sl = it % 2
                P.wait("scalar", "eT", mmv)
                P.wr("scalar", f"qsil{sl}")
                fn = Act.Silu if silu else Act.Identity
                e1 = P.op("scalar", lambda E, bk=bk, sl=sl, fn=fn: E.activation(
                    out=qsil[sl][:], in_=pb[bk][:, 0:512], func=fn), "eS")
                P.red(BK[bk], "eS", e1)
                P.wrote(f"qsil{sl}", "eS", e1)
                if rot:
                    P.rd("tensor", f"qsil{sl}")
                    P.wr("tensor", BK[2 + bk])
                    sw = P.op("tensor", lambda E, bk=bk, sl=sl: E.matmul(
                        pb[2 + bk][:, 0:512], permT_s[:], qsil[sl][:],
                        start=True, stop=True), "eT")
                    P.rd("vector", f"qsil{sl}")
                    P.op("vector", lambda E, sl=sl, tt=tt: E.tensor_mul(
                        out=r1_s[:], in0=qsil[sl][:], in1=crot_s[:, tt * 512:(tt + 1) * 512]))
                    P.wait("vector", "eT", sw)
                    r2v = P.op("vector", lambda E, bk=bk, tt=tt: E.tensor_mul(
                        out=r2_s[:], in0=pb[2 + bk][:, 0:512],
                        in1=srot_s[:, tt * 512:(tt + 1) * 512]), "eV")
                    P.red(BK[2 + bk], "eV", r2v)
                    P.red(f"qsil{sl}", "eV", r2v)
                    P.wr("vector", f"rout{sl}")
                    ro = P.op("vector", lambda E, sl=sl: E.tensor_add(
                        out=rout[sl][:], in0=r1_s[:], in1=r2_s[:]), "eV")
                    P.wrote(f"rout{sl}", "eV", ro)
                    P.rd("scalar", f"rout{sl}")
                    dm = P.dma("scalar", dst_sp[tt, g], rout[sl][:], f"aR{sl}")
                    P.red(f"rout{sl}", f"aR{sl}", dm)
                else:
                    dm = P.dma("scalar", dst_sp[tt, g], qsil[sl][:], f"aR{sl}")
                    P.red(f"qsil{sl}", f"aR{sl}", dm)
                it += 1

    proj_phase(wq, q_sp, rot=True, silu=True)
    proj_phase(wk, k_sp, rot=True, silu=True)
    k_spilled = (P.now("aR0"), P.now("aR1"))
    if PHASES <= 2:
        return _finalize(nc, P, SEMS)
    proj_phase(wv, vf_sp, rot=False, silu=False)
    v_spilled = (P.now("aR0"), P.now("aR1"))

    AR3.reset(0)
    kc_t = [AR3.take([128, DM], BF16, f"kc{i}") for i in range(2)]
    vc_t = [AR3.take([128, DM], BF16, f"vc{i}") for i in range(2)]
    kdt_t = AR3.take([128, DM], BF16, "kdt")
    vtt_t = AR3.take([128, DM], BF16, "vtt")

    def chunk_stream_a(sp, dst, key, c):
        tt, ci = c // 4, c % 4
        P.wr("sync", key)
        v = P.dma("sync", v3(dst, NF, 128),
                  sp[tt].rearrange("f p t -> p f t")[:, :, ci * 128:(ci + 1) * 128],
                  f"sC{c % 2}")
        P.wrote(key, f"sC{c % 2}", v)
        return v

    P.wait("sync", "aR0", v_spilled[0])
    P.wait("sync", "aR1", v_spilled[1])
    chunk_stream_a(k_sp, kc_t[0], "kc0", 0)
    chunk_stream_a(vf_sp, vc_t[0], "vc0", 0)

    for c in range(NCH):
        if c + 1 < NCH:
            chunk_stream_a(k_sp, kc_t[(c + 1) % 2], f"kc{(c + 1) % 2}", c + 1)
            chunk_stream_a(vf_sp, vc_t[(c + 1) % 2], f"vc{(c + 1) % 2}", c + 1)
        drip_mean(4)
        P.rd("tensor", f"kc{c % 2}")
        P.rd("tensor", f"vc{c % 2}")
        for G in range(2):
            pfk = "pfA" if G == 0 else "pfB"
            pft = pf[G]
            P.wr("tensor", pfk)
            tk = None
            for hh in range(8):
                h = G * 8 + hh
                tk = P.op("tensor", lambda E, h=h, hh=hh, pft=pft, c=c: E.transpose(
                    pft[:, hh * 128:(hh + 1) * 128],
                    kc_t[c % 2][:, h * 128:(h + 1) * 128], idb_s[:]),
                    "eT" if hh == 7 else None)
            P.wrote(pfk, "eT", tk)
            P.rd("vector", pfk)
            P.wr("vector", f"kdt{G}")
            kdv = None
            for hh in range(8):
                h = G * 8 + hh
                kdv = P.op("vector", lambda E, h=h, hh=hh, pft=pft: E.tensor_scalar_mul(
                    out=kdt_t[:, h * 128:(h + 1) * 128],
                    in0=pft[:, hh * 128:(hh + 1) * 128], scalar1=dkt_s[:, h:h + 1]),
                    "eV" if hh == 7 else None)
            P.red(pfk, "eV", kdv)
            P.wrote(f"kdt{G}", "eV", kdv)
            P.wr("tensor", pfk)
            tv = None
            for hh in range(8):
                h = G * 8 + hh
                tv = P.op("tensor", lambda E, h=h, hh=hh, pft=pft, c=c: E.transpose(
                    pft[:, hh * 128:(hh + 1) * 128],
                    vc_t[c % 2][:, h * 128:(h + 1) * 128], idb_s[:]),
                    "eT" if hh == 7 else None)
            P.wrote(pfk, "eT", tv)
            P.rd("scalar", pfk)
            P.wr("scalar", f"vtt{G}")
            for hh in range(8):
                h = G * 8 + hh
                cpv = P.op("scalar", lambda E, h=h, hh=hh, pft=pft: E.copy(
                    out=vtt_t[:, h * 128:(h + 1) * 128],
                    in_=pft[:, hh * 128:(hh + 1) * 128]), "eS")
            P.red(pfk, "eS", cpv)
            P.wrote(f"vtt{G}", "eS", cpv)
            P.rd("tensor", f"kdt{G}")
            P.rd("tensor", f"vtt{G}")
            for half in range(2):
                bkU = 4 + half
                P.wr("tensor", BK[bkU])
                uT = None
                for j in range(4):
                    h = G * 8 + half * 4 + j
                    uT = P.op("tensor", lambda E, h=h, j=j, bkU=bkU: E.matmul(
                        pb[bkU][:, j * 128:(j + 1) * 128],
                        kdt_t[:, h * 128:(h + 1) * 128],
                        vtt_t[:, h * 128:(h + 1) * 128], start=True, stop=True),
                        "eT" if j == 3 else None)
                P.wait("vector", "eT", uT)
                us = None
                for j in range(4):
                    h = G * 8 + half * 4 + j
                    us = P.op("vector", lambda E, h=h, j=j, bkU=bkU, lam=lam_c[h]:
                              E.scalar_tensor_tensor(
                                  out=acc_s[:, h * 128:(h + 1) * 128],
                                  in0=acc_s[:, h * 128:(h + 1) * 128], scalar=lam,
                                  in1=pb[bkU][:, j * 128:(j + 1) * 128],
                                  op0=Alu.mult, op1=Alu.add),
                              "eV" if j == 3 else None)
                P.red(BK[bkU], "eV", us)
        P.red(f"kc{c % 2}", "eT", tv)
        P.red(f"vc{c % 2}", "eT", tv)
        P.wait("scalar", "eV", kdv)
        kd_dma = P.dma("scalar", kd_sp[c], kdt_t[:], "aK")
        vt_dma = P.dma("scalar", vt_sp[c], vtt_t[:], "aV")
        P.red("kdt0", "aK", kd_dma); P.red("kdt1", "aK", kd_dma)
        P.red("vtt0", "aV", vt_dma); P.red("vtt1", "aV", vt_dma)
    acc_final = P.now("eV")
    kv_spilled = (P.now("aK"), P.now("aV"))
    if PHASES <= 3:
        return _finalize(nc, P, SEMS)

    AR3.reset(ar3_scr)
    g1t = [AR3.take([128, 512], BF16, f"g1t{i}") for i in range(2)]
    gout = [AR3.take([128, 512], BF16, f"gout{i}") for i in range(2)]
    P.wr("sync", "land0"); P.wr("sync", "land1")
    ldg1 = P.dma("sync", v3(land[0], NF, 128),
                 wg1.rearrange("(c p) g2 -> p c g2", p=128), "sB0")
    ldg2 = P.dma("sync", land[1][:, 0:2048], wg2, "sB1")
    P.wait("vector", "sB0", ldg1)
    P.wait("vector", "sB1", ldg2)
    P.wr("vector", "wbf0"); P.wr("vector", "wbf1")
    cg1 = P.op("vector", lambda E: E.tensor_copy(out=wbf[0][:], in_=land[0][:]), "eV")
    cg2 = P.op("vector", lambda E: E.tensor_copy(out=wbf[1][:], in_=land[1][:]), "eV")
    P.wait("tensor", "eV", cg2)
    for tt in range(4):
        bk = tt % 2
        P.wr("tensor", BK[bk])
        mmv = None
        for f in range(NF):
            mmv = P.op("tensor", lambda E, f=f, tt=tt, bk=bk: E.matmul(
                pb[bk][:, 0:512], wbf[0][:, f * 128:(f + 1) * 128],
                v3(h_sb, NF, T)[:, f, tt * 512:(tt + 1) * 512],
                start=(f == 0), stop=(f == NF - 1)),
                "eT" if f == NF - 1 else None)
        P.wait("scalar", "eT", mmv)
        P.wr("scalar", f"g1t{tt % 2}")
        g1e = P.op("scalar", lambda E, bk=bk, tt=tt: E.copy(
            out=g1t[tt % 2][:], in_=pb[bk][:, 0:512]), "eS")
        P.red(BK[bk], "eS", g1e)
        P.wrote(f"g1t{tt % 2}", "eS", g1e)
        P.rd("tensor", f"g1t{tt % 2}")
        for g2i in range(NF):
            bk2 = 2 + g2i % 2
            P.wr("tensor", BK[bk2])
            mg = P.op("tensor", lambda E, g2i=g2i, tt=tt, bk2=bk2: E.matmul(
                pb[bk2][:, 0:512], wbf[1][:, g2i * 128:(g2i + 1) * 128],
                g1t[tt % 2][:], start=True, stop=True), "eT")
            if g2i == NF - 1:
                P.red(f"g1t{tt % 2}", "eT", mg)
            P.wait("scalar", "eT", mg)
            P.wr("scalar", f"gout{g2i % 2}")
            g2e = P.op("scalar", lambda E, g2i=g2i, bk2=bk2: E.copy(
                out=gout[g2i % 2][:], in_=pb[bk2][:, 0:512]), "eS")
            P.red(BK[bk2], "eS", g2e)
            gdm = P.dma("scalar", g_sp[tt, g2i], gout[g2i % 2][:], f"aR{g2i % 2}")
            P.red(f"gout{g2i % 2}", f"aR{g2i % 2}", gdm)
        drip_mean(16)
    g_spilled = (P.now("aR0"), P.now("aR1"))
    if PHASES <= 4:
        return _finalize(nc, P, SEMS)

    P.wait("gpsimd", "eV", acc_final)
    if acc_dbg is not None:
        P.dma("gpsimd", acc_dbg, acc_s[:], "qG")
    sdma = P.dma("gpsimd", ccs_in, acc_s[:], "qG")
    P.wait("gpsimd", "qG", sdma)
    cc2 = P.op("gpsimd", lambda E: E.collective_compute(
        "AllGather", Alu.bypass, replica_groups=[[0, 1, 2, 3], [4, 5, 6, 7]],
        ins=[ccs_in[:].opt()], outs=[ccs_out[:].opt()]), "cc")
    if PHASES <= 4.2:
        P.wait("gpsimd", "cc", cc2)
        return _finalize(nc, P, SEMS)

    gat_t = at4(0x0000, [128, H * CH], F32, "gat_t")
    stw_t = at4(0x2000, [128, H * CH], F32, "stw_t")
    woland = at4(0x4000, [128, DM], F32, "woland")
    P.wait("sync", "cc", cc2)
    P.wait("sync", "eT", P.now("eT"))
    P.wait("sync", "eV", P.now("eV"))
    P.wait("sync", "eS", P.now("eS"))
    si_mark = None
    for j in range(4):
        if si_mark is not None:
            P.wait("sync", "eV", si_mark)
        l1 = P.dma("sync", gat_t[:], ccs_out[j], "sO")
        l2 = P.dma("sync", stw_t[:], stwb_t[j], "sO")
        P.wait("vector", "sO", l2)
        if j == 0:
            si_mark = P.op("vector", lambda E: E.tensor_mul(
                out=acc_s[:], in0=gat_t[:], in1=stw_t[:]), "eV")
        else:
            P.op("vector", lambda E: E.tensor_mul(out=gat_t[:], in0=gat_t[:], in1=stw_t[:]))
            si_mark = P.op("vector", lambda E: E.tensor_add(
                out=acc_s[:], in0=acc_s[:], in1=gat_t[:]), "eV")
    sb_cp = P.op("vector", lambda E: E.tensor_copy(out=sbf_s[:], in_=acc_s[:]), "eV")
    P.wrote("sbf", "eV", sb_cp)
    dtm_s = at4(0x0000, [128, H * CH], F32, "dtm_s")
    dqt_s = at4(0x2000, [128, H * CH], F32, "dqt_s")
    P.wait("sync", "eV", sb_cp)
    ld_dtm = P.dma("sync", dtm_s[:], dt_mask_t, "sO")
    ld_dtm = P.dma("sync", dqt_s[:], dqt_t, "sO")
    if si_dbg is not None:
        P.wait("gpsimd", "eV", sb_cp)
        P.dma("gpsimd", si_dbg, acc_s[:], "qG")
    if PHASES <= 4.5:
        return _finalize(nc, P, SEMS)

    AR1.reset()
    wo_sb = AR1.take([128, NF * DM], BF16, "wo_sb")
    wo_cast = None
    for f in range(NF):
        if wo_cast is not None:
            P.wait("sync", "eV", wo_cast)
        lw = P.dma("sync", woland[:], wo_d[f * 128:(f + 1) * 128, :], "sO")
        P.wait("vector", "sO", lw)
        if f == 0:
            P.rd("vector", "h")
            P.wait("vector", "eT", P.now("eT"))
        wo_cast = P.op("vector", lambda E, f=f: E.tensor_copy(
            out=v3(wo_sb, NF, DM)[:, f, :], in_=woland[:]), "eV")
    P.wrote("wo_sb", "eV", wo_cast)
    if PHASES <= 5:
        return _finalize(nc, P, SEMS)

    AR2.reset()
    qc_t = [AR2.take([128, DM], BF16, f"qc{i}") for i in range(2)]
    kc2_t = [AR2.take([128, DM], BF16, f"kc2{i}") for i in range(2)]
    AR3.reset()
    kdc_t = [AR3.take([128, DM], BF16, f"kdc{i}") for i in range(2)]
    vtc_t = [AR3.take([128, DM], BF16, f"vtc{i}") for i in range(2)]
    gc_t = [AR3.take([128, DM], BF16, f"gc{i}") for i in range(2)]
    og_t = AR3.take([128, DM], BF16, "og_t")
    qd_t = AR3.take([128, DM], BF16, "qd_t")
    AR5.reset()
    onrm = AR5.take([128, NF * 512], BF16, "onrm")

    def c_stream(c):
        tt, ci = c // 4, c % 4
        sl = c % 2
        for key, dst, sp3 in ((f"qc{sl}", qc_t[sl], q_sp), (f"kc2{sl}", kc2_t[sl], k_sp),
                              (f"gc{sl}", gc_t[sl], g_sp)):
            P.wr("sync", key)
            v = P.dma("sync", v3(dst, NF, 128),
                      sp3[tt].rearrange("f p t -> p f t")[:, :, ci * 128:(ci + 1) * 128],
                      f"sC{sl}")
            P.wrote(key, f"sC{sl}", v)
        for key, dst, sp2 in ((f"kdc{sl}", kdc_t[sl], kd_sp), (f"vtc{sl}", vtc_t[sl], vt_sp)):
            P.wr("sync", key)
            v = P.dma("sync", dst[:], sp2[c], f"sC{sl}")
            P.wrote(key, f"sC{sl}", v)
        return v

    P.wait("sync", "aR0", g_spilled[0])
    P.wait("sync", "aR1", g_spilled[1])
    P.wait("sync", "aK", kv_spilled[0])
    P.wait("sync", "aV", kv_spilled[1])
    c_stream(0)
    for blk in range(4):
        for ci4 in range(4):
            c = blk * 4 + ci4
            sl = c % 2
            if c + 1 < NCH:
                c_stream(c + 1)
            drip_mean(8)
            P.rd("gpsimd", f"qc{sl}")
            P.wr("gpsimd", "qd")
            qd_v = P.op("gpsimd", lambda E, sl=sl: E.tensor_mul(
                out=qd_t[:], in0=qc_t[sl][:], in1=dqt_s[:]), "eG")
            P.wrote("qd", "eG", qd_v)
            P.rd("scalar", f"gc{sl}")
            sg_v = P.op("scalar", lambda E, sl=sl: E.activation(
                out=gc_t[sl][:], in_=gc_t[sl][:], func=Act.Sigmoid), "eS")
            P.rd("tensor", f"kc2{sl}")
            P.rd("tensor", f"kdc{sl}")
            P.rd("tensor", f"vtc{sl}")
            oT_marks = []
            for G in range(4):
                bkS = G % 2
                bkO = 2 + G % 2
                bkU = 4 + G % 2
                scb = scb0 if G % 2 == 0 else scb1
                P.wr("tensor", BK[bkS])
                scT = None
                for j in range(4):
                    h = G * 4 + j
                    scT = P.op("tensor", lambda E, h=h, j=j, sl=sl, bkS=bkS: E.matmul(
                        pb[bkS][:, j * 128:(j + 1) * 128],
                        kc2_t[sl][:, h * 128:(h + 1) * 128],
                        qc_t[sl][:, h * 128:(h + 1) * 128], start=True, stop=True),
                        "eT" if j == 3 else None)
                P.wait("vector", "eT", scT)
                P.wr("vector", f"scb{G % 2}")
                scv = P.op("vector", lambda E, G=G, bkS=bkS, scb=scb: E.tensor_mul(
                    out=scb[:], in0=pb[bkS][:, 0:512],
                    in1=dtm_s[:, G * 512:(G + 1) * 512]), "eV")
                P.red(BK[bkS], "eV", scv)
                P.wrote(f"scb{G % 2}", "eV", scv)
                P.rd("tensor", f"scb{G % 2}")
                if G == 0:
                    P.rd("tensor", "qd")
                    P.rd("tensor", "sbf")
                P.wr("tensor", BK[bkO])
                oT = None
                for j in range(4):
                    h = G * 4 + j
                    P.op("tensor", lambda E, h=h, j=j, sl=sl, bkO=bkO, scb=scb: E.matmul(
                        pb[bkO][:, j * 128:(j + 1) * 128],
                        vtc_t[sl][:, h * 128:(h + 1) * 128],
                        scb[:, j * 128:(j + 1) * 128], start=True, stop=False))
                    oT = P.op("tensor", lambda E, h=h, j=j, bkO=bkO: E.matmul(
                        pb[bkO][:, j * 128:(j + 1) * 128],
                        sbf_s[:, h * 128:(h + 1) * 128],
                        qd_t[:, h * 128:(h + 1) * 128], start=False, stop=True),
                        "eT" if j == 3 else None)
                oT_marks.append(oT)
                P.red(f"scb{G % 2}", "eT", oT)
                P.wr("tensor", BK[bkU])
                uT = None
                for j in range(4):
                    h = G * 4 + j
                    uT = P.op("tensor", lambda E, h=h, j=j, sl=sl, bkU=bkU: E.matmul(
                        pb[bkU][:, j * 128:(j + 1) * 128],
                        kdc_t[sl][:, h * 128:(h + 1) * 128],
                        vtc_t[sl][:, h * 128:(h + 1) * 128], start=True, stop=True),
                        "eT" if j == 3 else None)
                P.wait("vector", "eT", oT)
                P.wait("vector", "eS", sg_v)
                P.wr("vector", f"og{G}")
                ogv = P.op("vector", lambda E, G=G, sl=sl, bkO=bkO: E.tensor_mul(
                    out=og_t[:, G * 512:(G + 1) * 512], in0=pb[bkO][:, 0:512],
                    in1=gc_t[sl][:, G * 512:(G + 1) * 512]), "eV")
                P.red(BK[bkO], "eV", ogv)
                P.wrote(f"og{G}", "eV", ogv)
                P.wait("vector", "eT", uT)
                for j in range(4):
                    h = G * 4 + j
                    P.op("vector", lambda E, h=h, j=j, bkU=bkU, lam=lam_c[h]:
                         E.scalar_tensor_tensor(
                             out=acc_s[:, h * 128:(h + 1) * 128],
                             in0=acc_s[:, h * 128:(h + 1) * 128], scalar=lam,
                             in1=pb[bkU][:, j * 128:(j + 1) * 128],
                             op0=Alu.mult, op1=Alu.add))
                P.wr("vector", "sbf")
                sbv = P.op("vector", lambda E, G=G: E.tensor_copy(
                    out=sbf_s[:, G * 512:(G + 1) * 512],
                    in_=acc_s[:, G * 512:(G + 1) * 512]), "eV")
                P.red(BK[bkU], "eV", sbv)
                P.wrote("sbf", "eV", sbv)
            lastT = uT
            P.red(f"qc{sl}", "eG", qd_v)
            P.red(f"gc{sl}", "eV", ogv)
            P.red(f"qc{sl}", "eT", lastT)
            P.red(f"kc2{sl}", "eT", lastT)
            P.red(f"kdc{sl}", "eT", lastT)
            P.red(f"vtc{sl}", "eT", lastT)
            P.red("qd", "eT", lastT)
            P.rd("scalar", "og0"); P.rd("scalar", "og1")
            P.rd("scalar", "og2"); P.rd("scalar", "og3")
            P.wr("scalar", "qd")
            sq_v2 = P.op("scalar", lambda E: E.activation(out=qd_t[:], in_=og_t[:],
                                                          func=Act.Square), "eS")
            P.wrote("qd", "eS", sq_v2)
            for Gx in range(4):
                P.red(f"og{Gx}", "eS", sq_v2)
            for jg in range(4):
                P.rd("tensor", "qd")
                P.wr("tensor", BK[0])
                m1 = P.op("tensor", lambda E, jg=jg: E.matmul(
                    pb[0][0:1, 0:512], ones_b[:], qd_t[:, jg * 512:(jg + 1) * 512],
                    start=True, stop=True), "eT")
                if jg == 3:
                    P.red("qd", "eT", m1)
                P.wait("scalar", "eT", m1)
                cpx = P.op("scalar", lambda E: E.copy(out=row_a[0:1, 0:512],
                                                      in_=pb[0][0:1, 0:512]), "eS")
                P.red(BK[0], "eS", cpx)
                P.op("scalar", lambda E: E.mul(out=row_a[0:1, 0:512], in_=row_a[0:1, 0:512],
                                               mul=1.0 / HD))
                P.op("scalar", lambda E: E.activation(out=row_a[0:1, 0:512],
                                                      in_=row_a[0:1, 0:512],
                                                      func=Act.Ln, bias=1e-6))
                rse = P.op("scalar", lambda E: E.activation(out=row_a[0:1, 0:512],
                                                            in_=row_a[0:1, 0:512],
                                                            func=Act.Exp, scale=-0.5), "eS")
                P.wait("tensor", "eS", rse)
                P.wr("tensor", BK[1])
                bcv = P.op("tensor", lambda E: E.matmul(
                    pb[1][:, 0:512], onesr_f[0:1, :], row_a[0:1, 0:512],
                    start=True, stop=True), "eT")
                P.wait("vector", "eT", bcv)
                P.rd("vector", f"og{jg}")
                P.wr("vector", "onrm")
                nv = None
                for j in range(4):
                    h = jg * 4 + j
                    nv = P.op("vector", lambda E, h=h, j=j, ci4=ci4: E.scalar_tensor_tensor(
                        out=v3(onrm, NF, 512)[:, h, ci4 * 128:(ci4 + 1) * 128],
                        in0=og_t[:, h * 128:(h + 1) * 128], scalar=nw_s[:, h:h + 1],
                        in1=pb[1][:, j * 128:(j + 1) * 128], op0=Alu.mult, op1=Alu.mult),
                        "eV" if j == 3 else None)
                P.red(BK[1], "eV", nv)
                P.red(f"og{jg}", "eV", nv)
        onrm_ready = P.now("eV")
        P.wrote("onrm", "eV", onrm_ready)
        P.rd("tensor", "onrm")
        P.rd("tensor", "wo_sb")
        for g2i in range(NF):
            bkW = 2 + g2i % 2
            P.wr("tensor", BK[bkW])
            wv_ = None
            for f in range(NF):
                wv_ = P.op("tensor", lambda E, f=f, g2i=g2i, bkW=bkW: E.matmul(
                    pb[bkW][:, 0:512], v3(wo_sb, NF, DM)[:, f, g2i * 128:(g2i + 1) * 128],
                    v3(onrm, NF, 512)[:, f, :], start=(f == 0), stop=(f == NF - 1)),
                    "eT" if f == NF - 1 else None)
            if g2i >= NF - 2:
                P.red("onrm", "eT", wv_)
            P.wait("vector", "eT", wv_)
            wslot = mb_s if g2i % 2 == 0 else rb_s
            P.wr("vector", f"woev{g2i % 2}")
            wev = P.op("vector", lambda E, bkW=bkW, ws=wslot: E.tensor_copy(
                out=ws[:], in_=pb[bkW][:, 0:512]), "eV")
            P.red(BK[bkW], "eV", wev)
            P.wrote(f"woev{g2i % 2}", "eV", wev)
            P.rd("scalar", f"woev{g2i % 2}")
            wod = P.dma("scalar", attn_sp[g2i, :, blk * 512:(blk + 1) * 512], wslot[:],
                        f"aW{g2i % 2}")
            P.red(f"woev{g2i % 2}", f"aW{g2i % 2}", wod)
    attn_spilled = (P.now("aW0"), P.now("aW1"))
    if PHASES <= 6:
        return _finalize(nc, P, SEMS)

    while mean_pos[0] < len(mean_items):
        drip_mean(8)
    P.wait("tensor", "eV", mean_marks[2])
    P.wait("tensor", "eV", mean_marks[1])
    P.wait("tensor", "eV", mean_marks[0])
    P.wr("tensor", BK[0])
    fm = P.op("tensor", lambda E: E.matmul(pb[0][0:1, 0:4], ones_f[:], macc[:],
                                           start=True, stop=True), "eT")
    P.wait("scalar", "eT", fm)
    cps = P.op("scalar", lambda E: E.copy(out=srow[0:1, :], in_=pb[0][0:1, 0:4]), "eS")
    P.red(BK[0], "eS", cps)
    sc2 = P.op("scalar", lambda E: E.mul(out=srow[0:1, 0:3], in_=srow[0:1, 0:3],
                                          mul=1.0 / (DM * DF)), "eS")
    P.wait("vector", "eS", sc2)
    clp = P.op("vector", lambda E: E.tensor_scalar(out=srow[0:1, 0:3], in0=srow[0:1, 0:3],
                                                   scalar1=1e-5, scalar2=None, op0=Alu.max), "eV")
    P.wait("tensor", "eV", clp)
    P.wr("tensor", BK[1])
    swv = None
    for j in range(3):
        swv = P.op("tensor", lambda E, j=j: E.matmul(pb[1][:, j:j + 1], onesr_f[0:1, :],
                                                     srow[0:1, j:j + 1], start=True, stop=True),
                   "eT" if j == 2 else None)
    P.wait("vector", "eT", swv)
    swcp = None
    for j, col in ((0, swg_c), (1, swu_c), (2, swd_c)):
        swcp = P.op("vector", lambda E, j=j, col=col: E.tensor_copy(
            out=col[:], in_=pb[1][:, j:j + 1]), "eV")
        P.wait("vector", "eV", swcp)
    P.red(BK[1], "eV", swcp)
    rw_done = None
    for col, rcol in ((swg_c, rwg_c), (swu_c, rwu_c), (swd_c, rwd_c)):
        rw_done = P.op("vector", lambda E, col=col, rcol=rcol: E.reciprocal(
            out=rcol[:], in_=col[:]), "eV")
        P.wait("vector", "eV", rw_done)
    gbv = P.op("vector", lambda E: E.tensor_scalar_mul(out=gbswd[:], in0=gb_v[:],
                                                       scalar1=swd_c[:]), "eV")

    AR2.reset()
    sqd = AR2.take([128, NF * 512], BF16, "sqd")
    AR3.reset()
    x1t = AR3.take([128, NF * 512], F32, "x1t")
    AR5.reset()
    h2q = AR5.take([128, NF * 512], BF16, "h2q")
    h2f = at4(0x0000, [128, NF * 512], F32, "h2f")
    gl_t = [at4(i * 0x1000, [128, 8 * 128], F32, f"gl{i}") for i in range(2)]
    ul_t = [at4(0x2000 + i * 0x1000, [128, 8 * 128], F32, f"ul{i}") for i in range(2)]
    glb_t = [at4(0x4000 + i * 0x800, [128, 8 * 128], BF16, f"glb{i}") for i in range(2)]
    ulb_t = [at4(0x5000 + i * 0x800, [128, 8 * 128], BF16, f"ulb{i}") for i in range(2)]
    dl_t = [at4(i * 0x2000, [128, 16 * 128], F32, f"dl{i}") for i in range(2)]
    dlb_t = [at4(0x4000 + i * 0x1000, [128, 16 * 128], FP16, f"dlb{i}") for i in range(2)]
    midq = None
    AR1.reset()
    midq = AR1.take([128, NO * 512], FP16, "midq")

    def colscale():
        P.wait("tensor", "eV", P.now("eV"))
        P.wait("tensor", "eS", P.now("eS"))
        rdv = None
        for j in range(4):
            P.wr("tensor", BK[0])
            tp = P.op("tensor", lambda E, j=j: E.transpose(
                pb[0][:, 0:128], amax_s[:, j * 128:(j + 1) * 128], idf_s[:]), "eT")
            P.wait("vector", "eT", tp)
            rdv = P.op("vector", lambda E, j=j: E.reduce_max(
                out=colsc[:, j:j + 1], in_=pb[0][:, 0:128], axis=AXX), "eV")
            P.red(BK[0], "eV", rdv)
        P.wait("vector", "eV", rdv)
        v0 = P.op("vector", lambda E: E.tensor_scalar(out=colsc[:, 0:4], in0=colsc[:, 0:4],
                                                      scalar1=1e-5, scalar2=None, op0=Alu.max),
                  "eV")
        P.wait("vector", "eV", v0)
        rcv = P.op("vector", lambda E: E.reciprocal(out=colsc[:, 4:8], in_=colsc[:, 0:4]),
                   "eV")
        P.wait("vector", "eV", rcv)
        v1 = P.op("vector", lambda E: E.tensor_scalar(out=colsc[:, 4:8], in0=colsc[:, 4:8],
                                                      scalar1=127.0, scalar2=None, op0=Alu.mult),
                  "eV")
        P.wait("vector", "eV", v1)
        cs = P.op("vector", lambda E: E.tensor_scalar(out=colsc[:, 0:4], in0=colsc[:, 0:4],
                                                      scalar1=1.0 / 127.0, scalar2=None,
                                                      op0=Alu.mult), "eV")
        P.wait("vector", "eV", cs)
        P.wait("tensor", "eV", cs)
        P.wr("tensor", BK[0])
        tp2 = P.op("tensor", lambda E: E.transpose(pb[0][0:8, 0:128], colsc[:], idf_s[:]), "eT")
        P.wait("scalar", "eT", tp2)
        cr = P.op("scalar", lambda E: E.copy(out=colrows[:], in_=pb[0][0:8, 0:128]), "eS")
        P.red(BK[0], "eS", cr)
        P.wait("gpsimd", "eS", cr)
        for j in range(4):
            P.dma("gpsimd", row_a[0:1, j * 128:(j + 1) * 128], colrows[4 + j:5 + j, :], "qG")
        d2 = None
        for j in range(4):
            d2 = P.dma("gpsimd", row_b[0:1, j * 128:(j + 1) * 128], colrows[j:j + 1, :], "qG")
        P.wait("tensor", "qG", d2)
        P.wr("tensor", BK[0]); P.wr("tensor", BK[1])
        b1 = P.op("tensor", lambda E: E.matmul(pb[0][:, 0:512], onesr_f[0:1, :],
                                               row_a[0:1, 0:512], start=True, stop=True), "eT")
        b2 = P.op("tensor", lambda E: E.matmul(pb[1][:, 0:512], onesr_f[0:1, :],
                                               row_b[0:1, 0:512], start=True, stop=True), "eT")
        P.wait("vector", "eT", b2)
        P.op("vector", lambda E: E.tensor_copy(out=sxb_s[:], in_=pb[0][:, 0:512]))
        v2_ = P.op("vector", lambda E: E.tensor_copy(out=sxi_s[:], in_=pb[1][:, 0:512]), "eV")
        P.red(BK[0], "eV", v2_); P.red(BK[1], "eV", v2_)
        return v2_

    P.wait("sync", "aW0", attn_spilled[0])
    P.wait("sync", "aW1", attn_spilled[1])
    P.wait("vector", "aW0", attn_spilled[0])
    P.wait("vector", "aW1", attn_spilled[1])
    for tt in range(4):
        P.wr("sync", "x1t")
        al = None
        for f in range(NF):
            al = P.dma("sync", v3(x1t, NF, 512)[:, f, :],
                       attn_sp[f, :, tt * 512:(tt + 1) * 512], "sX")
        P.wrote("x1t", "sX", al)
        x1_done = None
        for f in range(NF):
            key = f"psc{f % 2}"
            slot = psc0 if f % 2 == 0 else psc1
            P.wr("sync", key)
            xl = P.dma("sync", slot[:], xT[f * 128:(f + 1) * 128, tt * 512:(tt + 1) * 512],
                       f"sP{f % 2}")
            P.wrote(key, f"sP{f % 2}", xl)
            P.rd("vector", key)
            if f == 0:
                P.rd("vector", "x1t")
                P.wait("vector", "eV", gbv)
            x1_done = P.op("vector", lambda E, f=f, slot=slot: E.scalar_tensor_tensor(
                out=v3(x1t, NF, 512)[:, f, :], in0=v3(x1t, NF, 512)[:, f, :],
                scalar=ga_v[:, f:f + 1], in1=slot[:], op0=Alu.mult, op1=Alu.add), "eV")
            P.red(key, "eV", x1_done)
        P.wr("vector", "h2q")
        cast2 = P.op("vector", lambda E: E.tensor_copy(out=h2q[:], in_=x1t[:]), "eV")
        P.wait("scalar", "eV", x1_done)
        P.wr("scalar", "sqd")
        sq2 = P.op("scalar", lambda E: E.activation(out=sqd[:], in_=x1t[:], func=Act.Square), "eS")
        P.wait("tensor", "eV", cast2)
        P.wait("tensor", "eS", sq2)
        P.wr("tensor", BK[4]); P.wr("tensor", BK[5])
        for f in range(NF):
            P.op("tensor", lambda E, f=f: E.matmul(pb[4][0:1, 0:512], ones_b[:],
                                                   v3(h2q, NF, 512)[:, f, :],
                                                   start=(f == 0), stop=(f == NF - 1)))
        st2 = None
        for f in range(NF):
            st2 = P.op("tensor", lambda E, f=f: E.matmul(pb[5][0:1, 0:512], ones_b[:],
                                                         v3(sqd, NF, 512)[:, f, :],
                                                         start=(f == 0), stop=(f == NF - 1)),
                       "eT" if f == NF - 1 else None)
        P.red("sqd", "eT", st2)
        P.red("h2q", "eT", st2)
        P.wait("scalar", "eT", st2)
        P.op("scalar", lambda E: E.copy(out=row_a[0:1, 0:512], in_=pb[4][0:1, 0:512]))
        cp2 = P.op("scalar", lambda E: E.copy(out=row_b[0:1, 0:512], in_=pb[5][0:1, 0:512]), "eS")
        P.red(BK[4], "eS", cp2); P.red(BK[5], "eS", cp2)
        P.op("scalar", lambda E: E.mul(out=row_a[0:1, 0:512], in_=row_a[0:1, 0:512], mul=1.0 / DM))
        ms2 = P.op("scalar", lambda E: E.activation(out=row_c[0:1, 0:512], in_=row_a[0:1, 0:512],
                                                    func=Act.Square), "eS")
        P.wait("vector", "eS", ms2)
        vv2 = P.op("vector", lambda E: E.scalar_tensor_tensor(
            out=row_b[0:1, 0:512], in0=row_b[0:1, 0:512], scalar=1.0 / DM,
            in1=row_c[0:1, 0:512], op0=Alu.mult, op1=Alu.subtract), "eV")
        P.wait("scalar", "eV", vv2)
        P.op("scalar", lambda E: E.activation(out=row_b[0:1, 0:512], in_=row_b[0:1, 0:512],
                                              func=Act.Ln, bias=1e-5))
        rs2 = P.op("scalar", lambda E: E.activation(out=row_b[0:1, 0:512], in_=row_b[0:1, 0:512],
                                                    func=Act.Exp, scale=-0.5), "eS")
        P.wait("tensor", "eS", rs2)
        P.wr("tensor", BK[4]); P.wr("tensor", BK[5])
        P.op("tensor", lambda E: E.matmul(pb[4][:, 0:512], onesr_f[0:1, :], row_a[0:1, 0:512],
                                          start=True, stop=True))
        bc2 = P.op("tensor", lambda E: E.matmul(pb[5][:, 0:512], onesr_f[0:1, :],
                                                row_b[0:1, 0:512], start=True, stop=True), "eT")
        P.wait("vector", "eT", bc2)
        P.wr("vector", "mbrb")
        P.op("vector", lambda E: E.tensor_copy(out=mb_s[:], in_=pb[4][:, 0:512]))
        cmb2 = P.op("vector", lambda E: E.tensor_copy(out=rb_s[:], in_=pb[5][:, 0:512]), "eV")
        P.red(BK[4], "eV", cmb2); P.red(BK[5], "eV", cmb2)
        amz = P.op("vector", lambda E: E.memset(amax_s[:], 0.0), "eV")
        P.wr("vector", "h2f")
        for f in range(NF):
            P.op("vector", lambda E, f=f: E.scalar_tensor_tensor(
                out=v3(h2f, NF, 512)[:, f, :], in0=v3(x1t, NF, 512)[:, f, :],
                scalar=1.0, in1=mb_s[:], op0=Alu.mult, op1=Alu.subtract))
            P.op("vector", lambda E, f=f: E.tensor_mul(
                out=v3(h2f, NF, 512)[:, f, :], in0=v3(h2f, NF, 512)[:, f, :], in1=rb_s[:]))
            hm = P.op("vector", lambda E, f=f: E.tensor_scalar(
                out=v3(h2f, NF, 512)[:, f, :], in0=v3(h2f, NF, 512)[:, f, :],
                scalar1=cb1[:, f:f + 1], scalar2=sb_v[:, f:f + 1],
                op0=Alu.mult, op1=Alu.add), "eV")
            P.wait("scalar", "eV", hm)
            slot = psc0 if f % 2 == 0 else psc1
            P.wr("scalar", f"psc{f % 2}")
            ab = P.op("scalar", lambda E, f=f, slot=slot: E.activation(
                out=slot[:], in_=v3(h2f, NF, 512)[:, f, :], func=Act.Abs), "eS")
            P.wrote(f"psc{f % 2}", "eS", ab)
            P.rd("vector", f"psc{f % 2}")
            amv = P.op("vector", lambda E, slot=slot: E.tensor_tensor(
                out=amax_s[:], in0=amax_s[:], in1=slot[:], op=Alu.max), "eV")
            P.red(f"psc{f % 2}", "eV", amv)
        csd = colscale()
        qdn = None
        P.wr("vector", "h2q")
        for f in range(NF):
            slot = psc0 if f % 2 == 0 else psc1
            if f == 0:
                P.wait("vector", "eV", csd)
            P.wr("vector", f"psc{f % 2}")
            P.op("vector", lambda E, f=f, slot=slot: E.tensor_mul(
                out=slot[:], in0=v3(h2f, NF, 512)[:, f, :], in1=sxb_s[:]))
            qdn = P.op("vector", lambda E, f=f, slot=slot: E.tensor_scalar(
                out=v3(h2q, NF, 512)[:, f, :], in0=slot[:],
                scalar1=MAGIC, scalar2=MAGIC, op0=Alu.add, op1=Alu.subtract), "eV")
        P.wrote("h2q", "eV", qdn)
        P.red("h2f", "eV", qdn)
        amz2 = P.op("vector", lambda E: E.memset(amax_s[:], 0.0), "eV")
        mid_done = None
        vq = gq = None
        for o in range(NO):
            bkG = o % 2
            bkU2 = 2 + o % 2
            P.wr("tensor", BK[bkG]); P.wr("tensor", BK[bkU2])
            for hb in range(2):
                sl = (o * 2 + hb) % 2
                P.wr("sync", f"gl{sl}")
                if o == 0 and hb == 0:
                    P.wait("sync", "eV", qdn)
                    P.wait("sync", "eS", P.now("eS"))
                l1 = P.dma("sync", v3(gl_t[sl], 8, 128),
                           wgT[hb * 1024:(hb + 1) * 1024, o * 128:(o + 1) * 128]
                           .rearrange("(c p) g2 -> p c g2", p=128), f"sG{sl}")
                P.wrote(f"gl{sl}", f"sG{sl}", l1)
                P.wr("sync", f"ul{sl}")
                l2 = P.dma("sync", v3(ul_t[sl], 8, 128),
                           wuT[hb * 1024:(hb + 1) * 1024, o * 128:(o + 1) * 128]
                           .rearrange("(c p) g2 -> p c g2", p=128), f"sG{sl}")
                P.wrote(f"ul{sl}", f"sG{sl}", l2)
                P.rd("vector", f"gl{sl}")
                P.wr("vector", f"glb{sl}")
                P.op("vector", lambda E, sl=sl: E.tensor_scalar(
                    out=gl_t[sl][:], in0=gl_t[sl][:], scalar1=rwg_c[:], scalar2=MAGIC,
                    op0=Alu.mult, op1=Alu.add))
                P.op("vector", lambda E, sl=sl: E.tensor_scalar(
                    out=gl_t[sl][:], in0=gl_t[sl][:], scalar1=MAGIC - 1.0, scalar2=MAGIC + 1.0,
                    op0=Alu.max, op1=Alu.min))
                vq = P.op("vector", lambda E, sl=sl: E.tensor_scalar(
                    out=glb_t[sl][:], in0=gl_t[sl][:], scalar1=MAGIC, scalar2=None,
                    op0=Alu.subtract), "eV")
                P.red(f"gl{sl}", "eV", vq)
                P.wrote(f"glb{sl}", "eV", vq)
                P.rd("gpsimd", f"ul{sl}")
                P.wr("gpsimd", f"ulb{sl}")
                if tt == 0 and o == 0 and hb == 0:
                    P.wait("gpsimd", "eV", gbv)
                P.op("gpsimd", lambda E, sl=sl: E.tensor_scalar(
                    out=ul_t[sl][:], in0=ul_t[sl][:], scalar1=rwu_c[:], scalar2=MAGIC,
                    op0=Alu.mult, op1=Alu.add))
                P.op("gpsimd", lambda E, sl=sl: E.tensor_scalar(
                    out=ul_t[sl][:], in0=ul_t[sl][:], scalar1=MAGIC - 1.0, scalar2=MAGIC + 1.0,
                    op0=Alu.max, op1=Alu.min))
                gq = P.op("gpsimd", lambda E, sl=sl: E.tensor_scalar(
                    out=ulb_t[sl][:], in0=ul_t[sl][:], scalar1=MAGIC, scalar2=None,
                    op0=Alu.subtract), "eG")
                P.red(f"ul{sl}", "eG", gq)
                P.wrote(f"ulb{sl}", "eG", gq)
                P.rd("tensor", f"glb{sl}")
                P.rd("tensor", f"ulb{sl}")
                if o == 0 and hb == 0:
                    P.rd("tensor", "h2q")
                for f in range(8):
                    fa = hb * 8 + f
                    P.op("tensor", lambda E, f=f, fa=fa, sl=sl, bkG=bkG, hb=hb: E.matmul(
                        pb[bkG][:, 0:512], glb_t[sl][:, f * 128:(f + 1) * 128],
                        v3(h2q, NF, 512)[:, fa, :],
                        start=(hb == 0 and f == 0), stop=(hb == 1 and f == 7)))
                guv = None
                for f in range(8):
                    fa = hb * 8 + f
                    guv = P.op("tensor", lambda E, f=f, fa=fa, sl=sl, bkU2=bkU2, hb=hb: E.matmul(
                        pb[bkU2][:, 0:512], ulb_t[sl][:, f * 128:(f + 1) * 128],
                        v3(h2q, NF, 512)[:, fa, :],
                        start=(hb == 0 and f == 0), stop=(hb == 1 and f == 7)),
                        "eT" if f == 7 else None)
                P.red(f"glb{sl}", "eT", guv)
                P.red(f"ulb{sl}", "eT", guv)
            last_guv = guv
            sl = bkG
            P.wait("vector", "eT", guv)
            P.wr("vector", "psc0"); P.wr("vector", "psc1")
            P.op("vector", lambda E, sl=sl: E.scalar_tensor_tensor(
                out=psc0[:], in0=pb[sl][:, 0:512], scalar=swg_c[:], in1=sxi_s[:],
                op0=Alu.mult, op1=Alu.mult))
            ue = P.op("vector", lambda E, sl=sl: E.scalar_tensor_tensor(
                out=psc1[:], in0=pb[2 + sl][:, 0:512], scalar=swu_c[:], in1=sxi_s[:],
                op0=Alu.mult, op1=Alu.mult), "eV")
            P.red(BK[sl], "eV", ue); P.red(BK[2 + sl], "eV", ue)
            P.wait("scalar", "eV", ue)
            se = P.op("scalar", lambda E: E.activation(out=psc0[:], in_=psc0[:],
                                                       func=Act.Silu), "eS")
            P.wait("vector", "eS", se)
            P.wr("vector", f"mid{o % 4}")
            pr = P.op("vector", lambda E, o=o: E.tensor_mul(
                out=v3(midq, NO, 512)[:, o, :], in0=psc0[:], in1=psc1[:]), "eV")
            P.wrote(f"mid{o % 4}", "eV", pr)
            P.rd("scalar", f"mid{o % 4}")
            ab2 = P.op("scalar", lambda E, o=o: E.activation(
                out=psc0[:], in_=v3(midq, NO, 512)[:, o, :], func=Act.Abs), "eS")
            P.red(f"mid{o % 4}", "eS", ab2)
            P.wait("vector", "eS", ab2)
            mid_done = P.op("vector", lambda E: E.tensor_tensor(
                out=amax_s[:], in0=amax_s[:], in1=psc0[:], op=Alu.max), "eV")
        P.red("h2q", "eT", last_guv)
        P.red("h2f", "eS", ab2)
        csd2 = colscale()
        mq_done = None
        for o in range(NO):
            if o == 0:
                P.wait("vector", "eV", csd2)
            P.wr("vector", f"mid{o % 4}")
            P.wr("vector", "psc0")
            P.op("vector", lambda E, o=o: E.tensor_mul(
                out=psc0[:], in0=v3(midq, NO, 512)[:, o, :], in1=sxb_s[:]))
            mq_done = P.op("vector", lambda E, o=o: E.tensor_scalar(
                out=v3(midq, NO, 512)[:, o, :], in0=psc0[:], scalar1=MAGIC, scalar2=MAGIC,
                op0=Alu.add, op1=Alu.subtract), "eV")
        P.wrote("midq", "eV", mq_done)
        y_dma = None
        for o2 in range(NF):
            bkD = 4 + o2 % 2
            P.wr("tensor", BK[bkD])
            for qq in range(4):
                sl = qq % 2
                P.wr("sync", f"dl{sl}")
                if o2 == 0 and qq == 0:
                    P.wait("sync", "eV", vq)
                    P.wait("sync", "eG", gq)
                    P.wait("sync", "eT", last_guv)
                ld = P.dma("sync", v3(dl_t[sl], 16, 128),
                           wdT[qq * 2048:(qq + 1) * 2048, o2 * 128:(o2 + 1) * 128]
                           .rearrange("(c p) g2 -> p c g2", p=128), f"sD{sl}")
                P.wrote(f"dl{sl}", f"sD{sl}", ld)
                P.rd("vector", f"dl{sl}")
                P.wr("vector", f"dlb{sl}")
                P.op("vector", lambda E, sl=sl: E.tensor_scalar(
                    out=dl_t[sl][:], in0=dl_t[sl][:], scalar1=rwd_c[:], scalar2=MAGIC,
                    op0=Alu.mult, op1=Alu.add))
                P.op("vector", lambda E, sl=sl: E.tensor_scalar(
                    out=dl_t[sl][:], in0=dl_t[sl][:], scalar1=MAGIC - 1.0, scalar2=MAGIC + 1.0,
                    op0=Alu.max, op1=Alu.min))
                dqv = P.op("vector", lambda E, sl=sl: E.tensor_scalar(
                    out=dlb_t[sl][:], in0=dl_t[sl][:], scalar1=MAGIC, scalar2=None,
                    op0=Alu.subtract), "eV")
                P.red(f"dl{sl}", "eV", dqv)
                P.wrote(f"dlb{sl}", "eV", dqv)
                P.rd("tensor", f"dlb{sl}")
                if o2 == 0 and qq == 0:
                    P.rd("tensor", "midq")
                dnv = None
                for i2 in range(16):
                    oi = qq * 16 + i2
                    dnv = P.op("tensor", lambda E, oi=oi, i2=i2, sl=sl, bkD=bkD: E.matmul(
                        pb[bkD][:, 0:512], dlb_t[sl][:, i2 * 128:(i2 + 1) * 128],
                        v3(midq, NO, 512)[:, oi, :],
                        start=(oi == 0), stop=(oi == NO - 1)),
                        "eT" if i2 == 15 else None)
                P.red(f"dlb{sl}", "eT", dnv)
            P.wait("vector", "eT", dnv)
            P.wr("vector", "psc0"); P.wr("vector", "psc1")
            P.op("vector", lambda E, o2=o2, bkD=bkD: E.scalar_tensor_tensor(
                out=psc0[:], in0=pb[bkD][:, 0:512], scalar=gbswd[:, o2:o2 + 1], in1=sxi_s[:],
                op0=Alu.mult, op1=Alu.mult))
            ye = P.op("vector", lambda E, o2=o2: E.tensor_add(
                out=psc1[:], in0=psc0[:], in1=v3(x1t, NF, 512)[:, o2, :]), "eV")
            P.red(BK[bkD], "eV", ye)
            if o2 == NF - 1:
                P.red("x1t", "eV", ye)
            P.rd("scalar", "psc1")
            P.wait("scalar", "eV", ye)
            y_dma = P.dma("scalar", yT[o2 * 128:(o2 + 1) * 128, tt * 512:(tt + 1) * 512],
                          psc1[:], "aY")
            P.red("psc1", "aY", y_dma)
        for mm4 in range(4):
            P.red(f"mid{mm4}", "eT", dnv)
        P.red("midq", "eT", dnv)
        if tt == 0 and f_dbg1 is not None:
            P.wait("gpsimd", "eV", P.now("eV"))
            P.wait("gpsimd", "eS", P.now("eS"))
            P.dma("gpsimd", f_dbg1, x1t[:], "qG")
            P.dma("gpsimd", f_dbg2, h2q[:], "qG")
            P.dma("gpsimd", f_dbg3, midq[:], "qG")
            P.dma("gpsimd", f_dbg4[:, 0:512], sxb_s[:], "qG")
            P.dma("gpsimd", f_dbg4[:, 512:1024], sxi_s[:], "qG")
            P.dma("gpsimd", f_dbg4[:, 1024:1536], amax_s[:], "qG")
            P.dma("gpsimd", f_dbg4[:, 1536:2048], mb_s[:], "qG")
            P.dma("gpsimd", f_dbg4[:, 2048:2049], swg_c[:], "qG", slow=True)
            P.dma("gpsimd", f_dbg4[:, 2049:2050], swu_c[:], "qG", slow=True)
            P.dma("gpsimd", f_dbg4[:, 2050:2051], swd_c[:], "qG", slow=True)
            P.dma("gpsimd", f_dbg4[:, 2051:2052], rwg_c[:], "qG", slow=True)
            P.dma("gpsimd", f_dbg4[:, 2052:2053], rwd_c[:], "qG", slow=True)
            P.dma("gpsimd", f_dbg4[:, 2053:2054], gbswd[:, 0:1], "qG", slow=True)
            P.dma("gpsimd", f_dbg4[:, 2054:2055], ga_v[:, 0:1], "qG", slow=True)
            P.dma("gpsimd", f_dbg4[:, 2055:2056], gb_v[:, 0:1], "qG", slow=True)
            gdbg = P.dma("gpsimd", f_dbg4[:, 2056:2057], ca1[:, 0:1], "qG", slow=True)
            P.wait("gpsimd", "qG", gdbg)
    return _finalize(nc, P, SEMS)


def _finalize(nc, P, SEMS):
    for sm in ("aR0", "aR1", "aK", "aV", "aW0", "aW1", "aY"):
        P.wait("scalar", sm, P.now(sm))
    for sm in ("sK", "sW0", "sW1", "sX", "sM0", "sM1", "sB0", "sB1", "sR",
               "sC0", "sC1", "sO", "sP0", "sP1", "sG0", "sG1", "sD0", "sD1"):
        P.wait("sync", sm, P.now(sm))
    P.wait("gpsimd", "qG", P.now("qG"))
    import contextlib
    with contextlib.ExitStack() as stack:
        for s in SEMS:
            P.sems[s] = stack.enter_context(nc.semaphore(s))
        block = stack.enter_context(nc.Block())
        for ename in ENGINES:
            fns = P.ops[ename]

            def runner(E, fns=fns):
                for f in fns:
                    f(E)
            getattr(block, ename)(runner)
    return nc


_NC = None


def _get_nc():
    global _NC
    if _NC is None:
        _NC = build_nc()
    return _NC


def make_in_maps(inputs):
    tab = _tables()
    x = np.asarray(inputs["L0_attn_x"])
    wq = np.asarray(inputs["L0_attn_Wq"]); wk = np.asarray(inputs["L0_attn_Wk"])
    wv = np.asarray(inputs["L0_attn_Wv"]); wo = np.asarray(inputs["L0_attn_Wo"])
    wg1 = np.asarray(inputs["L0_attn_Wg1"]); wg2 = np.asarray(inputs["L0_attn_Wg2"])
    nw = np.asarray(inputs["L0_attn_norm_weight"])
    w_gate = np.asarray(inputs["L0_ffn_w_gate"]); w_up = np.asarray(inputs["L0_ffn_w_up"])
    w_down = np.asarray(inputs["L0_ffn_w_down"])
    cc = np.asarray(inputs["L0_c"]); wmod = np.asarray(inputs["L0_wmod"])

    wgT = np.ascontiguousarray(w_gate.T)
    wuT = np.ascontiguousarray(w_up.T)
    wdT = np.ascontiguousarray(w_down.T)
    nwt = np.ascontiguousarray(nw.reshape(NF, 128).T)
    cst_host = np.zeros((128, 8), np.float32)
    for j, val in enumerate(CONSTS):
        cst_host[:, j] = np.float32(val)
    in_maps = []
    for c in range(8):
        b, p = c // 4, c % 4
        t0 = p * T
        m = {
            "xT": np.ascontiguousarray(x[b, t0:t0 + T, :].T),
            "wq": wq, "wk": wk, "wv": wv, "wo": wo, "wg1": wg1, "wg2": wg2,
            "nwt": nwt, "wgT": wgT, "wuT": wuT, "wdT": wdT,
            "cvec": np.ascontiguousarray(cc[b].reshape(NF, 128).T),
            "wmod_sl": np.ascontiguousarray(wmod[:, p * 3072:(p + 1) * 3072]),
            "crot_t": np.ascontiguousarray(tab["crot"][:, t0:t0 + T]),
            "srot_t": np.ascontiguousarray(tab["srot"][:, t0:t0 + T]),
            "dt_mask_t": tab["dt_mask"], "dqt_t": tab["dqt"], "dkt_t": tab["dkt"],
            "permT_t": tab["permT"], "idf_t": tab["ident"], "idb_t": tab["identb"],
            "stwb_t": np.ascontiguousarray(tab["stwb"][p]),
            "cstt": cst_host,
        }
        in_maps.append(m)
    return in_maps


def kernel(**inputs):
    nc = _get_nc()
    in_maps = make_in_maps(inputs)
    res = run_bass_kernel_spmd(nc, in_maps, core_ids=list(range(8)))
    y = np.empty((2, N, DM), np.float32)
    for c in range(8):
        b, p = c // 4, c % 4
        y[b, p * T:(p + 1) * T, :] = res.results[c]["yT"].T
    return y
